# version 01
# baseline (speedup 1.0000x reference)
import numpy as np
import ml_dtypes
from contextlib import ExitStack

import concourse.bass as bass
import concourse.mybir as mybir
from concourse.bass_core_rust import IndirectOffsetOnAxis
from concourse.bass_utils import run_bass_kernel_spmd

F32 = mybir.dt.float32
BF16 = mybir.dt.bfloat16
I32 = mybir.dt.int32
AF = mybir.ActivationFunctionType
ALU = mybir.AluOpType

B, N, D, H, CH = 2, 4096, 2048, 8192, 256
NT = 1024
DC = 16
HC = 64
SBH = 8
NSB = HC // SBH
NPOS = 4
SL = [4, 8, 12, 16]
CSETS = [[r, 7 - r, 8 + r, 15 - r] for r in range(4)]
EPS1 = 1e-5
EPS2 = 1e-8
EPSF = float(np.finfo(np.float32).eps)
CMAGIC = 12582912.0
CC_RANK_ROWS = 9 * 128
CC_ROWS = 4 * CC_RANK_ROWS
ENGS = ("tensor", "vector", "scalar", "gpsimd", "sync")
SEMS = ("xl", "wl0", "wl1", "wd2", "aw0", "aw1", "il", "kv0", "kv1", "vv0", "vv1",
        "cw", "cw0", "cw1", "cc", "mmS", "ev", "sc", "st")
A32_ELS = 8448
ABF_ELS = 32768


class Arena:
    def __init__(self, t, total):
        self.t = t
        self.total = total
        self.off = 0

    def mark(self):
        return self.off

    def reset(self, m):
        self.off = m

    def take(self, shape):
        n = int(np.prod(shape[1:]))
        o = self.off
        assert o + n <= self.total, f"arena overflow {o}+{n}>{self.total}"
        self.off = o + n
        ap = self.t[0:shape[0], o:o + n]
        if len(shape) == 2:
            return ap
        if len(shape) == 3:
            return ap.rearrange("p (a b) -> p a b", a=shape[1])
        if len(shape) == 4:
            return ap.rearrange("p (a b c) -> p a b c", a=shape[1], b=shape[2])
        raise ValueError(shape)


class MK:

    NATIVE = {"vector": "ev", "scalar": "sc"}
    SERIAL = {"aw0", "aw1", "kv0", "kv1", "vv0", "vv1"}

    def __init__(self, nc):
        self.nc = nc
        self.plan = {e: [] for e in ENGS}
        self.cnt = {}
        self.sems = {}
        self.last_selfwait = {}

    def E(self, eng, fn):
        self.plan[eng].append(fn)

    def inc(self, sem, n):
        self.cnt[sem] = self.cnt.get(sem, 0) + n
        return self.cnt[sem]

    def cur(self, sem):
        return self.cnt.get(sem, 0)

    def wait(self, eng, sem, val):
        if val and val > 0:
            if self.NATIVE.get(eng) == sem:
                if val <= self.last_selfwait.get(eng, 0):
                    return
                self.last_selfwait[eng] = val
            self.E(eng, lambda e, s=sem, v=val: e.wait_ge(self.sems[s], v))

    def dma(self, eng, out, in_, sem):
        if sem in self.SERIAL:
            self.wait(eng, sem, self.cur(sem))
        v = self.inc(sem, 16)
        self.E(eng, lambda e, o=out, i=in_, s=sem:
               e.dma_start(out=o, in_=i).then_inc(self.sems[s], 16))
        return v

    def idma(self, out, in_, idx, sem):
        if sem in self.SERIAL:
            self.wait("gpsimd", sem, self.cur(sem))
        v = self.inc(sem, 16)
        self.E("gpsimd", lambda e, o=out, i=in_, x=idx, s=sem:
               e.indirect_dma_start(o, None, i, IndirectOffsetOnAxis(x, 0))
               .then_inc(self.sems[s], 16))
        return v

    def op(self, eng, meth, *args, sem=None, **kw):
        nat = self.NATIVE.get(eng)
        if nat is not None:
            cv = self.cur(nat)
            if cv > self.last_selfwait.get(eng, 0):
                self.wait(eng, nat, cv)
            if sem is None:
                sem = nat
        v = self.inc(sem, 1) if sem else None
        def fn(e, m=meth, a=args, s=sem, k=kw):
            ins = getattr(e, m)(*a, **k)
            if s:
                ins.then_inc(self.sems[s], 1)
        self.E(eng, fn)
        return v

    def mm(self, out, lhsT, rhs, start, stop, sem=None):
        v = self.inc(sem, 1) if sem else None
        def fn(e, o=out, l=lhsT, r=rhs, st=start, sp=stop, s=sem):
            ins = e.matmul(o, l, r, start=st, stop=sp)
            if s:
                ins.then_inc(self.sems[s], 1)
        self.E("tensor", fn)
        return v

    def transpose(self, out, in_, ident, sem):
        v = self.inc(sem, 1)
        self.E("tensor", lambda e, o=out, i=in_, d=ident, s=sem:
               e.transpose(o, i, d).then_inc(self.sems[s], 1))
        return v


def build_program(stages=("ffn1", "attn", "ffn2")):
    nc = bass.Bass("TRN2", target_bir_lowering=False, debug=False)
    dr = {}
    dr["xt"] = nc.dram_tensor("xt", [DC, 128, NT], F32, kind="ExternalInput").ap()
    for nm in ("wq", "wk", "wga", "wp"):
        dr[nm] = nc.dram_tensor(nm, [16, 128, 16, 128], F32, kind="ExternalInput").ap()
    dr["wv"] = nc.dram_tensor("wv", [4, 16, 128, 512], F32, kind="ExternalInput").ap()
    dr["wg"] = nc.dram_tensor("wg", [HC, 128, DC, 128], F32, kind="ExternalInput").ap()
    dr["wu"] = nc.dram_tensor("wu", [HC, 128, DC, 128], F32, kind="ExternalInput").ap()
    dr["wd"] = nc.dram_tensor("wd", [HC, 128, D], F32, kind="ExternalInput").ap()
    dr["mask"] = nc.dram_tensor("mask", [128, 2, 256], F32, kind="ExternalInput").ap()
    dr["kvidx"] = nc.dram_tensor("kvidx", [128, NPOS, 16, 2], I32, kind="ExternalInput").ap()
    dr["ident"] = nc.dram_tensor("ident", [128, 128], F32, kind="ExternalInput").ap()
    dr["out"] = nc.dram_tensor("out", [DC, 128, NT], F32, kind="ExternalOutput").ap()
    dr["cc_in"] = nc.dram_tensor("cc_in", [CC_RANK_ROWS, 4096], BF16, kind="Internal").ap()
    dr["cc_out"] = nc.dram_tensor("cc_out", [CC_ROWS, 4096], BF16, kind="Internal").ap()
    dr["wcg"] = nc.dram_tensor("wcg", [HC, 128, D], BF16, kind="Internal").ap()
    dr["wcu"] = nc.dram_tensor("wcu", [HC, 128, D], BF16, kind="Internal").ap()
    dr["wcd"] = nc.dram_tensor("wcd", [HC, 128, D], BF16, kind="Internal").ap()
    dr["qsc"] = nc.dram_tensor("qsc", [128, DC, NT], BF16, kind="Internal").ap()
    dr["gsc"] = nc.dram_tensor("gsc", [128, DC, NT], BF16, kind="Internal").ap()

    es = ExitStack()
    sb = {}

    def alloc(name, shape, dt):
        sb[name] = es.enter_context(nc.sbuf_tensor(name, shape, dt))
        return sb[name]

    alloc("X", [128, DC, NT], F32)
    alloc("QN", [128, DC, NT], BF16)
    alloc("onesb", [128, 1], BF16)
    alloc("ones1", [1, 128], F32)
    alloc("zbf", [128, 128], BF16)
    alloc("CONSTP", [128, 8], F32)
    alloc("CONSTR", [1, 12], F32)
    alloc("MASK", [128, 2, 256], F32)
    alloc("IDX", [128, NPOS, 16, 2], I32)
    alloc("GS", [128, 2, 512], BF16)
    alloc("IDB", [128, 128], F32)
    alloc("AF", [128, A32_ELS], F32)
    alloc("AB", [128, ABF_ELS], BF16)
    a32 = Arena(sb["AF"], A32_ELS)
    abf = Arena(sb["AB"], ABF_ELS)

    mk = MK(nc)
    op, dma, mm, wait, cur = mk.op, mk.dma, mk.mm, mk.wait, mk.cur
    X, QN = sb["X"], sb["QN"]
    PC = {"C": 0, "mC": 1, "neg1": 2, "half": 3, "cliplo": 4, "cliphi": 5}
    RC = {"invn": 0, "e2m": 1, "e2a": 2, "efm": 3, "efa": 4, "c127": 5, "i127": 6,
          "neg1r": 7}
    cp = lambda k: sb["CONSTP"][:, PC[k]:PC[k] + 1]
    c1 = lambda k: sb["CONSTR"][0:1, RC[k]:RC[k] + 1]

    for j in range(4):
        dma("sync", X[:, 4 * j:4 * j + 4, :], dr["xt"][4 * j:4 * j + 4].transpose([1, 0, 2]), "xl")
    dma("sync", sb["MASK"][:], dr["mask"], "xl")
    x_loaded = dma("sync", sb["IDX"][:], dr["kvidx"], "xl")
    idb_loaded = dma("gpsimd", sb["IDB"][:], dr["ident"], "il")
    op("vector", "memset", sb["onesb"][:], 1.0)
    op("vector", "memset", sb["ones1"][:], 1.0)
    op("vector", "memset", sb["zbf"][:], 0.0)
    for k, vv in (("C", CMAGIC), ("mC", -CMAGIC), ("neg1", -1.0), ("half", 0.5),
                  ("cliplo", -128.0), ("cliphi", 127.0)):
        op("vector", "memset", cp(k), vv)
    for k, vv in (("invn", 1.0 / D), ("e2m", 1.0 + EPS2), ("e2a", EPS1 * EPS2),
                  ("efm", 1.0 + EPSF), ("efa", EPS1 * EPSF),
                  ("c127", 127.0), ("i127", 1.0 / 127.0), ("neg1r", -1.0)):
        op("vector", "memset", c1(k), vv)
    op("vector", "memset", sb["GS"][:], 0.0, sem="ev")

    pb = [es.enter_context(nc.psum_tensor(f"pb{k}", [128, 512], F32)) for k in range(8)]

    def norm_phase(kind, ready):
        m32, mbf = a32.mark(), abf.mark()
        XB = abf.take([128, 2, NT])
        X2B = abf.take([128, 2, NT])
        MACC = abf.take([128, 2, NT])
        ROWS = a32.take([1, 4, NT])
        TQ = a32.take([128, NT])
        SRb = a32.take([128, NT])
        MSRb = a32.take([128, NT])
        IVb = a32.take([128, NT])
        MCOL = a32.take([128, 16])
        M16 = a32.take([16, 128])
        ffn = kind == "ffn"
        for e in ENGS:
            for sm, vl in ready:
                wait(e, sm, vl)
        stat_mm, tree_ev = {}, {}
        for i in range(DC):
            par = i % 2
            if i >= 2:
                wait("vector", "mmS", stat_mm[i - 2])
                wait("scalar", "mmS", stat_mm[i - 2])
            cast_ev = op("vector", "tensor_copy", XB[:, par, :], X[:, i, :])
            sq_sc = op("scalar", "activation", X2B[:, par, :], X[:, i, :], AF.Square)
            wait("tensor", "ev", cast_ev)
            wait("tensor", "sc", sq_sc)
            st, sp = i == 0, i == DC - 1
            mm(pb[0][0:1, :], sb["onesb"][:], XB[:, par, 0:512], st, sp)
            mm(pb[1][0:1, :], sb["onesb"][:], XB[:, par, 512:1024], st, sp)
            mm(pb[2][0:1, :], sb["onesb"][:], X2B[:, par, 0:512], st, sp)
            stat_mm[i] = mm(pb[3][0:1, :], sb["onesb"][:], X2B[:, par, 512:1024], st, sp, sem="mmS")
            if ffn:
                if i == 0:
                    op("vector", "tensor_copy", MACC[:, 0, :], XB[:, par, :])
                    tree_ev[i] = op("vector", "tensor_copy", MACC[:, 1, :], XB[:, par, :])
                else:
                    op("vector", "tensor_tensor", MACC[:, 0, :], MACC[:, 0, :],
                       XB[:, par, :], ALU.max)
                    tree_ev[i] = op("vector", "tensor_tensor", MACC[:, 1, :], MACC[:, 1, :],
                                    XB[:, par, :], ALU.min)
        Rm, Rr, Ra, Rb_ = 0, 1, 2, 3
        r = lambda k: ROWS[0:1, k, :]
        last_stat = stat_mm[DC - 1]
        wait("scalar", "mmS", last_stat)
        op("scalar", "copy", r(Ra)[:, 0:512], pb[0][0:1, :])
        op("scalar", "copy", r(Ra)[:, 512:1024], pb[1][0:1, :])
        op("scalar", "copy", r(Rb_)[:, 0:512], pb[2][0:1, :])
        sc_st = op("scalar", "copy", r(Rb_)[:, 512:1024], pb[3][0:1, :])
        wait("vector", "sc", sc_st)
        op("vector", "tensor_scalar_mul", r(Rm), r(Ra), c1("invn"))
        op("vector", "tensor_scalar_mul", r(Rb_), r(Rb_), c1("invn"))
        op("vector", "tensor_mul", r(Ra), r(Rm), r(Rm))
        op("vector", "tensor_sub", r(Rb_), r(Rb_), r(Ra))
        vv = op("vector", "tensor_scalar", r(Rb_), r(Rb_),
                c1("e2m" if ffn else "efm"), c1("e2a" if ffn else "efa"),
                ALU.mult, ALU.add)
        wait("scalar", "ev", vv)
        sq2 = op("scalar", "activation", r(Ra), r(Rb_), AF.Sqrt)
        wait("vector", "sc", sq2)
        op("vector", "reciprocal", r(Rr), r(Ra))
        if ffn:
            wait("tensor", "il", idb_loaded)
            for w in range(2):
                tcast = op("vector", "tensor_copy", TQ[:], MACC[:, w, :])
                wait("tensor", "ev", tcast)
                for j in range(8):
                    tg = mk.transpose(pb[6 + w][:, 0:128], TQ[:, j * 128:(j + 1) * 128],
                                      sb["IDB"][:], "mmS")
                    wait("vector", "mmS", tg)
                    rd = op("vector", "tensor_reduce", MCOL[:, w * 8 + j:w * 8 + j + 1],
                            pb[6 + w][:, 0:128], mybir.AxisListType.X,
                            ALU.max if w == 0 else ALU.min)
                    wait("tensor", "ev", rd)
            tg = mk.transpose(pb[6][0:16, 0:128], MCOL[:], sb["IDB"][:], "mmS")
            wait("vector", "mmS", tg)
            cx = op("vector", "tensor_copy", M16[:], pb[6][0:16, 0:128])
            wait("sync", "ev", cx)
            dma("sync", r(Ra), M16[0:8, :], "xl")
            rr_ld = dma("sync", r(Rb_), M16[8:16, :], "xl")
            wait("vector", "xl", rr_ld)
            op("vector", "tensor_sub", r(Ra), r(Ra), r(Rm))
            op("vector", "tensor_scalar_mul", r(Rb_), r(Rb_), c1("neg1r"))
            op("vector", "tensor_add", r(Rb_), r(Rb_), r(Rm))
            op("vector", "tensor_max", r(Ra), r(Ra), r(Rb_))
            op("vector", "tensor_mul", r(Ra), r(Ra), r(Rr))
            op("vector", "reciprocal", r(Rb_), r(Ra))
            op("vector", "tensor_scalar_mul", r(Rb_), r(Rb_), c1("c127"))
            op("vector", "tensor_mul", r(Rb_), r(Rb_), r(Rr))
            op("vector", "tensor_mul", r(Rm), r(Rm), r(Rb_))
            op("vector", "tensor_scalar_mul", r(Ra), r(Ra), c1("i127"))
            brows = [(Rb_, SRb), (Rm, MSRb), (Ra, IVb)]
        else:
            op("vector", "tensor_mul", r(Rm), r(Rm), r(Rr))
            brows = [(Rr, SRb), (Rm, MSRb)]
        rows_done = cur("ev")
        wait("tensor", "ev", rows_done)
        for rk, dst in brows:
            mm(pb[6][:], sb["ones1"][:], ROWS[0:1, rk, 0:512], True, True)
            g = mm(pb[7][:], sb["ones1"][:], ROWS[0:1, rk, 512:1024], True, True, sem="mmS")
            wait("vector", "mmS", g)
            op("vector", "tensor_copy", dst[:, 0:512], pb[6][:])
            bev = op("vector", "tensor_copy", dst[:, 512:1024], pb[7][:])
            wait("tensor", "ev", bev)
        last = None
        for i in range(DC):
            if ffn:
                op("vector", "tensor_mul", TQ[:], X[:, i, :], SRb[:])
                tev = op("vector", "tensor_sub", TQ[:], TQ[:], MSRb[:])
                wait("scalar", "ev", tev)
                op("scalar", "activation", TQ[:], TQ[:], AF.Identity, bias=cp("C"))
                tsc = op("scalar", "activation", TQ[:], TQ[:], AF.Identity, bias=cp("mC"))
                wait("vector", "sc", tsc)
                op("vector", "tensor_scalar", TQ[:], TQ[:],
                   cp("cliplo"), cp("cliphi"), ALU.max, ALU.min)
                last = op("vector", "tensor_tensor", QN[:, i, :], TQ[:], IVb[:], ALU.mult)
            else:
                op("vector", "tensor_mul", TQ[:], X[:, i, :], SRb[:])
                last = op("vector", "tensor_sub", QN[:, i, :], TQ[:], MSRb[:])
        a32.reset(m32)
        abf.reset(mbf)
        return [("ev", last)]

    def ffn_main(ready, first_pass, cache_cw=None):
        m32, mbf = a32.mark(), abf.mark()
        if first_pass:
            WGST = a32.take([128, DC, 128])
            WUST = a32.take([128, DC, 128])
            WDST = a32.take([128, D])
        SWG = abf.take([128, 2, DC, 128])
        SWU = abf.take([128, 2, DC, 128])
        SWD = abf.take([128, SBH, D])
        P = abf.take([128, SBH, 2, 512])
        wl_h, sgn_h, upG, upU, pmul = {}, {}, {}, {}, {}
        cwr = {}
        down_mm, xadd = {}, {}
        for e in ENGS:
            for sm, vl in ready:
                wait(e, sm, vl)
        if not first_pass and cache_cw is not None:
            wait("sync", "cw0", cache_cw[0])
            wait("sync", "cw1", cache_cw[1])
        last_x = None
        for h in range(HC):
            sbi, hj = h // SBH, h % SBH
            par = h % 2
            wlS = "wl0" if par == 0 else "wl1"
            cwS = "cw0" if par == 0 else "cw1"
            if first_pass:
                if h >= 1:
                    wait("sync", "sc", sgn_h[h - 1])
                dma("sync", WGST[:], dr["wg"][h], wlS)
                dma("sync", WUST[:], dr["wu"][h], wlS)
                wl_h[h] = dma("sync", WDST[:], dr["wd"][h], wlS)
                wait("scalar", wlS, wl_h[h])
                if h >= 2:
                    wait("scalar", "mmS", upU[h - 2])
                    wait("scalar", cwS, cwr[h - 2])
                if hj == 0 and sbi >= 1:
                    wait("scalar", "mmS", down_mm[sbi - 1])
                op("scalar", "sign", SWG[:, par].rearrange("p a b -> p (a b)"),
                   WGST[:].rearrange("p a b -> p (a b)"))
                op("scalar", "sign", SWU[:, par].rearrange("p a b -> p (a b)"),
                   WUST[:].rearrange("p a b -> p (a b)"))
                sgn_h[h] = op("scalar", "sign", SWD[:, hj, :], WDST[:])
                wait("sync", "sc", sgn_h[h])
                dma("sync", dr["wcg"][h], SWG[:, par].rearrange("p a b -> p (a b)"), cwS)
                dma("sync", dr["wcu"][h], SWU[:, par].rearrange("p a b -> p (a b)"), cwS)
                cwr[h] = dma("sync", dr["wcd"][h], SWD[:, hj, :], cwS)
            else:
                if h >= 2:
                    wait("sync", "mmS", upU[h - 2])
                if hj == 0 and sbi >= 1:
                    wait("sync", "mmS", down_mm[sbi - 1])
                dma("sync", SWG[:, par].rearrange("p a b -> p (a b)"), dr["wcg"][h], wlS)
                wl_h[h] = dma("sync", SWU[:, par].rearrange("p a b -> p (a b)"), dr["wcu"][h], wlS)
                if hj == 0:
                    for j2 in range(SBH):
                        wd_sb = dma("sync", SWD[:, j2, :], dr["wcd"][sbi * SBH + j2], "wd2")
            if first_pass:
                wait("tensor", "sc", sgn_h[h])
            else:
                wait("tensor", wlS, wl_h[h])
                if hj == 0:
                    wait("tensor", "wd2", wd_sb)
            for th in range(2):
                if h >= 1:
                    wait("tensor", "ev", pmul[(h - 1, th)])
                for dc in range(DC):
                    upG[(h, th)] = mm(pb[0 + th][:], SWG[:, par, dc, :],
                                      QN[:, dc, th * 512:th * 512 + 512],
                                      dc == 0, dc == DC - 1,
                                      sem="mmS" if dc == DC - 1 else None)
                for dc in range(DC):
                    upU[h] = mm(pb[2 + th][:], SWU[:, par, dc, :],
                                QN[:, dc, th * 512:th * 512 + 512],
                                dc == 0, dc == DC - 1,
                                sem="mmS" if dc == DC - 1 else None)
            for th in range(2):
                wait("scalar", "mmS", upG[(h, th)])
                if h >= 1:
                    wait("scalar", "ev", pmul[(h - 1, th)])
                gsil = op("scalar", "activation", sb["GS"][:, th, :], pb[0 + th][:], AF.Silu)
                wait("vector", "sc", gsil)
                wait("vector", "mmS", upU[h])
                if sbi >= 1 and hj == 0:
                    wait("vector", "mmS", down_mm[sbi - 1])
                pmul[(h, th)] = op("vector", "tensor_tensor", P[:, hj, th, :],
                                   sb["GS"][:, th, :], pb[2 + th][:], ALU.mult)
            if hj < SBH - 1:
                continue
            prev_grp = None
            for dq in range(4):
                for th in range(2):
                    wait("tensor", "ev", pmul[(sbi * SBH + SBH - 1, th)])
                    if prev_grp is not None:
                        wait("tensor", "ev", prev_grp)
                    elif sbi >= 1:
                        wait("tensor", "ev", xadd[(sbi - 1, 3, 1)])
                    g = None
                    for hj2 in range(SBH):
                        for j in range(4):
                            g = mm(pb[4 + j][:],
                                   SWD[:, hj2, (dq * 4 + j) * 128:(dq * 4 + j) * 128 + 128],
                                   P[:, hj2, th, :], hj2 == 0, hj2 == SBH - 1,
                                   sem="mmS" if (hj2 == SBH - 1 and j == 3) else None)
                    wait("vector", "mmS", g)
                    for j in range(4):
                        xadd[(sbi, dq, th)] = op(
                            "vector", "scalar_tensor_tensor",
                            X[:, dq * 4 + j, th * 512:th * 512 + 512], pb[4 + j][:],
                            cp("half"), X[:, dq * 4 + j, th * 512:th * 512 + 512],
                            ALU.mult, ALU.add)
                    prev_grp = xadd[(sbi, dq, th)]
            down_mm[sbi] = cur("mmS")
            last_x = xadd[(sbi, 3, 1)]
        a32.reset(m32)
        abf.reset(mbf)
        return [("ev", last_x), ("cw0", cur("cw0")), ("cw1", cur("cw1"))]

    def attn_phase(ready):
        qa = norm_phase("attn", ready)

        m32p, mbfp = a32.mark(), abf.mark()
        WPRJ = abf.take([128, 2, 16, 128])
        KVW = abf.take([128, DC, NT])
        WPRJV = abf.take([128, 2, 8, 512])
        KVWf = KVW.rearrange("p a b -> p (a b)")

        for e in ENGS:
            for sm, vl in qa:
                wait(e, sm, vl)
        zev = op("vector", "memset", KVW[:, 0:4, :], 0.0)
        wait("sync", "ev", zev)
        zcw = dma("sync", dr["cc_in"][8 * 128:9 * 128, :], KVW[:, 0:4, :], "cw")

        def wproj(name, dst, func, bank0, pre_mm=None, pre_sc=None, pre_load=None):
            pmm, psc = {}, {-1: 0, -2: 0}
            for oc in range(16):
                par = oc % 2
                if oc >= 2:
                    wait("gpsimd", "mmS", pmm[oc - 2])
                elif pre_load is not None:
                    wait("gpsimd", pre_load[0], pre_load[1])
                aw = dma("gpsimd", WPRJ[:, par], dr[name][oc],
                         "aw0" if par == 0 else "aw1")
                wait("tensor", "aw0" if par == 0 else "aw1", aw)
                if oc == 0 and pre_mm is not None:
                    wait("tensor", pre_mm[0], pre_mm[1])
                if oc >= 2:
                    wait("tensor", "sc", psc[oc - 2])
                g = None
                for th in range(2):
                    bank = bank0 + par * 2 + th
                    for dc in range(DC):
                        g = mm(pb[bank][:], WPRJ[:, par, dc, :],
                               QN[:, dc, th * 512:th * 512 + 512], dc == 0, dc == DC - 1,
                               sem="mmS" if (th == 1 and dc == DC - 1) else None)
                pmm[oc] = g
                wait("scalar", "mmS", g)
                if oc == 0 and pre_sc is not None:
                    wait("scalar", pre_sc[0], pre_sc[1])
                for th in range(2):
                    bank = bank0 + par * 2 + th
                    psc[oc] = op("scalar", "activation", dst[:, oc, th * 512:th * 512 + 512],
                                 pb[bank][:], func)
            return pmm[15], psc[15]

        kmm, ksc = wproj("wk", KVW, AF.Silu, 0, pre_sc=("cw", zcw))
        wait("sync", "sc", ksc)
        kcw = None
        for c in range(4):
            kcw = dma("sync", dr["cc_in"][c * 128:(c + 1) * 128, :],
                      KVW[:, :, c * 256:(c + 1) * 256], "cw")
        vmm, vsc = {}, {}
        for eg in range(4):
            for hh in range(2):
                if (eg, hh) != (0, 0):
                    prev = vmm[(eg, hh - 1)] if hh else vmm[(eg - 1, 1)]
                    wait("gpsimd", "mmS", prev)
                aw = dma("gpsimd", WPRJV[:, hh], dr["wv"][eg, hh * 8:hh * 8 + 8].transpose([1, 0, 2]),
                         "aw0" if hh == 0 else "aw1")
                wait("tensor", "aw0" if hh == 0 else "aw1", aw)
                if hh == 0:
                    wait("tensor", "sc", vsc[eg - 1] if eg else ksc)
                g = None
                for dcs in range(8):
                    dc = hh * 8 + dcs
                    for t in range(8):
                        g = mm(pb[t][:], QN[:, dc, t * 128:(t + 1) * 128], WPRJV[:, hh, dcs, :],
                               dc == 0, dc == 15,
                               sem="mmS" if (dcs == 7 and t == 7) else None)
                vmm[(eg, hh)] = g
            wait("scalar", "mmS", vmm[(eg, 1)])
            if eg == 0:
                wait("scalar", "cw", kcw)
            for t in range(8):
                c, u = t // 2, t % 2
                off = ((c * 2 + u) * 4 + eg) * 512
                vsc[eg] = op("scalar", "activation", KVWf[:, off:off + 512], pb[t][:], AF.Silu)
        wait("sync", "sc", vsc[3])
        for c in range(4):
            dma("sync", dr["cc_in"][(4 + c) * 128:(5 + c) * 128, :],
                KVWf[:, c * 4096:(c + 1) * 4096], "cw")
        wait("gpsimd", "cw", cur("cw"))
        ccv = mk.inc("cc", 1)
        mk.E("gpsimd", lambda e: e.collective_compute(
            "AllGather", ALU.bypass, ins=[dr["cc_in"]], outs=[dr["cc_out"]],
            replica_groups=[[0, 1, 2, 3], [4, 5, 6, 7]]).then_inc(mk.sems["cc"], 1))
        qmm, qev = wproj("wq", KVW, AF.Silu, 0, pre_mm=("sc", vsc[3]), pre_load=("mmS", vmm[(3, 1)]))
        wait("sync", "sc", qev)
        qcw = dma("sync", dr["qsc"], KVW[:], "cw")
        gmm, gev = wproj("wga", KVW, AF.Sigmoid, 4, pre_sc=("cw", qcw), pre_load=("mmS", qmm))
        wait("sync", "sc", gev)
        gcw = dma("sync", dr["gsc"], KVW[:], "cw")
        a32.reset(m32p)
        abf.reset(mbfp)

        m32t, mbft = a32.mark(), abf.mark()
        QBs = abf.take([128, 16, 256])
        GBs = abf.take([128, 16, 256])
        KS = abf.take([128, 2, 16, 256])
        VS = abf.take([128, 2, 2, 2048])
        SS = abf.take([128, 16, 2, 256])
        KSf = [KS[:, q2].rearrange("p a b -> p (a b)") for q2 in range(2)]
        VSf = [VS[:, q2].rearrange("p a b -> p (a b)") for q2 in range(2)]
        wait("gpsimd", "cc", ccv)
        wait("gpsimd", "cw", gcw)
        wait("sync", "cw", gcw)
        smm, sev, omm = {}, {}, {}
        oev = {-1: ("sc", gev)}
        for p in range(NPOS):
            L = SL[p]
            wait("sync", oev[p - 1][0], oev[p - 1][1])
            qb_cw = dma("sync", QBs[:], dr["qsc"][:, :, p * 256:(p + 1) * 256], "cw")
            gb_cw = dma("sync", GBs[:], dr["gsc"][:, :, p * 256:(p + 1) * 256], "cw")
            wait("tensor", "cw", gb_cw)
            for s in range(L):
                par = s % 2
                if s >= 2:
                    wait("gpsimd", "mmS", smm[(p, s - 2)])
                kv_k = mk.idma(KSf[par], dr["cc_out"], sb["IDX"][:, p, s, 0:1],
                               "kv0" if par == 0 else "kv1")
                wait("tensor", "kv0" if par == 0 else "kv1", kv_k)
                if s == 0:
                    wait("tensor", oev[p - 1][0], oev[p - 1][1])
                else:
                    wait("tensor", sev[(p, s - 1)][0], sev[(p, s - 1)][1])
                g = None
                for ju in range(2):
                    for dc in range(DC):
                        g = mm(pb[6 + ju][:, 0:256], KS[:, par, dc, ju * 128:(ju + 1) * 128],
                               QBs[:, dc, :], dc == 0, dc == DC - 1,
                               sem="mmS" if (ju == 1 and dc == DC - 1) else None)
                smm[(p, s)] = g
                if s == 0:
                    wait("vector", "mmS", g)
                    op("vector", "tensor_tensor", SS[:, s, 0, :], pb[6][:, 0:256],
                       sb["MASK"][:, 0, :], ALU.mult)
                    e2 = op("vector", "tensor_tensor", SS[:, s, 1, :], pb[7][:, 0:256],
                            sb["MASK"][:, 1, :], ALU.mult)
                    sev[(p, s)] = ("ev", e2)
                else:
                    wait("scalar", "mmS", g)
                    op("scalar", "copy", SS[:, s, 0, :], pb[6][:, 0:256])
                    e2 = op("scalar", "copy", SS[:, s, 1, :], pb[7][:, 0:256])
                    sev[(p, s)] = ("sc", e2)
            wait("tensor", sev[(p, L - 1)][0], sev[(p, L - 1)][1])
            if p >= 1:
                wait("tensor", oev[p - 1][0], oev[p - 1][1])
            for j in range(8):
                mm(pb[j][:], sb["zbf"][:], sb["GS"][:, 0, :], True, False)
            for s in range(L):
                par = s % 2
                if s >= 2:
                    wait("gpsimd", "mmS", omm[(p, s - 2)])
                kv_v = mk.idma(VSf[par], dr["cc_out"], sb["IDX"][:, p, s, 1:2],
                               "vv0" if par == 0 else "vv1")
                wait("tensor", "vv0" if par == 0 else "vv1", kv_v)
                g = None
                for e in range(16):
                    for u in range(2):
                        g = mm(pb[e // 2][:, (e % 2) * 256:(e % 2) * 256 + 256],
                               VS[:, par, u, e * 128:(e + 1) * 128], SS[:, s, u, :],
                               False, s == L - 1 and u == 1 and e % 2 == 1,
                               sem="mmS" if (e == 15 and u == 1) else None)
                omm[(p, s)] = g
            wait("vector", "mmS", omm[(p, L - 1)])
            wait("vector", "cw", gb_cw)
            e2 = None
            for e in range(16):
                e2 = op("vector", "tensor_tensor", QN[:, e, p * 256:(p + 1) * 256],
                        pb[e // 2][:, (e % 2) * 256:(e % 2) * 256 + 256],
                        GBs[:, e, :], ALU.mult)
            oev[p] = ("ev", e2)
        a32.reset(m32t)
        abf.reset(mbft)

        m32w, mbfw = a32.mark(), abf.mark()
        WPRJ2 = abf.take([128, 2, 16, 128])
        wpmm, wpev = {}, {}
        wait("gpsimd", oev[3][0], oev[3][1])
        for oc in range(16):
            par = oc % 2
            if oc >= 2:
                wait("gpsimd", "mmS", wpmm[oc - 2])
            aw = dma("gpsimd", WPRJ2[:, par], dr["wp"][oc],
                     "aw0" if par == 0 else "aw1")
            wait("tensor", "aw0" if par == 0 else "aw1", aw)
            if oc == 0:
                wait("tensor", oev[3][0], oev[3][1])
            if oc >= 2:
                wait("tensor", "ev", wpev[oc - 2])
            g = None
            for th in range(2):
                bank = par * 2 + th
                for ec in range(DC):
                    g = mm(pb[bank][:], WPRJ2[:, par, ec, :], QN[:, ec, th * 512:th * 512 + 512],
                           ec == 0, ec == DC - 1,
                           sem="mmS" if (th == 1 and ec == DC - 1) else None)
            wpmm[oc] = g
            wait("vector", "mmS", g)
            for th in range(2):
                bank = par * 2 + th
                wpev[oc] = op("vector", "tensor_add", X[:, oc, th * 512:th * 512 + 512],
                              X[:, oc, th * 512:th * 512 + 512], pb[bank][:])
        a32.reset(m32w)
        abf.reset(mbfw)
        return [("ev", wpev[15])]

    barrier = [("xl", x_loaded)]
    ffn1_cw = (0, 0)
    if "ffn1" in stages:
        q1 = norm_phase("ffn", barrier)
        barrier = ffn_main(q1, True)
        ffn1_cw = (cur("cw0"), cur("cw1"))
    if "attn" in stages:
        barrier = attn_phase(barrier)
    if "ffn2" in stages:
        q2 = norm_phase("ffn", barrier)
        barrier = ffn_main(q2, "ffn1" not in stages, cache_cw=ffn1_cw if "ffn1" in stages else None)

    for sm, vl in barrier:
        wait("sync", sm, vl)
    for j in range(4):
        dma("sync", dr["out"][4 * j:4 * j + 4].transpose([1, 0, 2]), X[:, 4 * j:4 * j + 4, :], "st")
    wait("sync", "st", cur("st"))

    with ExitStack() as ses:
        for s in SEMS:
            mk.sems[s] = ses.enter_context(nc.semaphore(s))
        with nc.Block() as block:
            for eng in ENGS:
                def mkrun(eng):
                    def runner(e):
                        for fn in mk.plan[eng]:
                            fn(e)
                    return runner
                getattr(block, eng)(mkrun(eng))
    es.close()
    return nc


def _chunk_owner():
    own = {}
    for r in range(4):
        for ci, t in enumerate(CSETS[r]):
            own[t] = (r, ci)
    return own


def _host_inputs(L0_attn_x, Wq, Wk, Wv, Wg, Wp, W_g, W_u, W_d):
    prep = {}
    tp = lambda w: np.ascontiguousarray(
        np.asarray(w).reshape(16, 128, 16, 128).transpose(0, 3, 2, 1))
    prep["wq"] = tp(Wq)
    prep["wk"] = tp(Wk)
    prep["wga"] = tp(Wg)
    prep["wp"] = tp(Wp)
    prep["wv"] = np.ascontiguousarray(
        np.asarray(Wv).reshape(4, 512, 16, 128).transpose(0, 2, 3, 1))
    prep["wg"] = np.ascontiguousarray(
        np.asarray(W_g).reshape(HC, 128, DC, 128).transpose(0, 3, 2, 1))
    prep["wu"] = np.ascontiguousarray(
        np.asarray(W_u).reshape(HC, 128, DC, 128).transpose(0, 3, 2, 1))
    prep["wd"] = np.ascontiguousarray(np.asarray(W_d).T.reshape(HC, 128, D))
    prep["ident"] = np.eye(128, dtype=np.float32)
    jj = np.arange(256)[:, None]
    ii = np.arange(256)[None, :]
    prep["mask"] = np.ascontiguousarray(
        (jj <= ii).astype(np.float32).reshape(2, 128, 256).transpose(1, 0, 2))
    own = _chunk_owner()
    x = np.asarray(L0_attn_x)
    in_maps = []
    for c in range(8):
        g, rr = c // 4, c % 4
        rows = np.concatenate([np.arange(t * CH, (t + 1) * CH) for t in CSETS[rr]])
        xs = x[g][rows]
        xt = np.ascontiguousarray(xs.T.reshape(DC, 128, NT))
        kvidx = np.zeros((128, NPOS, 16, 2), np.int32)
        pvec = np.arange(128, dtype=np.int32)
        for p in range(NPOS):
            t = CSETS[rr][p]
            for s in range(16):
                if s == 0:
                    tp_ = t
                elif s - 1 < t and s < SL[p]:
                    tp_ = s - 1
                else:
                    tp_ = -1
                if tp_ < 0:
                    kb = vb = 8 * 128
                else:
                    ro, ci = own[tp_]
                    kb = ro * CC_RANK_ROWS + ci * 128
                    vb = ro * CC_RANK_ROWS + (4 + ci) * 128
                kvidx[:, p, s, 0] = kb + pvec
                kvidx[:, p, s, 1] = vb + pvec
        m = dict(prep)
        m["xt"] = xt
        m["kvidx"] = kvidx
        in_maps.append(m)
    return in_maps


_NC_CACHE = {}


def kernel(L0_attn_x, L0_attn_Wq, L0_attn_Wk, L0_attn_Wv, L0_attn_Wg, L0_attn_Wp,
           L0_ffn_W_g, L0_ffn_W_u, L0_ffn_W_d, stages=("ffn1", "attn", "ffn2")):
    in_maps = _host_inputs(L0_attn_x, L0_attn_Wq, L0_attn_Wk, L0_attn_Wv,
                           L0_attn_Wg, L0_attn_Wp, L0_ffn_W_g, L0_ffn_W_u, L0_ffn_W_d)
    key = tuple(stages)
    if key not in _NC_CACHE:
        _NC_CACHE[key] = build_program(stages)
    nc = _NC_CACHE[key]
    res = run_bass_kernel_spmd(nc, in_maps, core_ids=list(range(8)))
    out = np.zeros((B, N, D), np.float32)
    for c in range(8):
        g, rr = c // 4, c % 4
        rows = np.concatenate([np.arange(t * CH, (t + 1) * CH) for t in CSETS[rr]])
        oc = np.asarray(res.results[c]["out"]).reshape(D, NT).T
        out[g][rows] = oc
    return out


# version 02
# speedup vs baseline: 1.0841x; 1.0841x over previous
import numpy as np
import ml_dtypes
from contextlib import ExitStack

import concourse.bass as bass
import concourse.mybir as mybir
from concourse.bass_core_rust import IndirectOffsetOnAxis
from concourse.bass_utils import run_bass_kernel_spmd

F32 = mybir.dt.float32
BF16 = mybir.dt.bfloat16
I32 = mybir.dt.int32
AF = mybir.ActivationFunctionType
ALU = mybir.AluOpType

B, N, D, H, CH = 2, 4096, 2048, 8192, 256
NT = 1024
DC = 16
HC = 64
SBH = 8
NSB = HC // SBH
NPOS = 4
SL = [4, 8, 12, 16]
CSETS = [[r, 7 - r, 8 + r, 15 - r] for r in range(4)]
EPS1 = 1e-5
EPS2 = 1e-8
EPSF = float(np.finfo(np.float32).eps)
CMAGIC = 12582912.0
CC_RANK_ROWS = 4 * 128
CC_ROWS = 8 * CC_RANK_ROWS + 128
ENGS = ("tensor", "vector", "scalar", "gpsimd", "sync")
SEMS = ("xl", "wl0", "wl1", "wd2", "aw0", "aw1", "il", "kv0", "kv1", "vv0", "vv1",
        "cw", "cw0", "cw1", "cwd", "cc", "mmS", "ev", "sc", "st")
A32_ELS = 8448
ABF_ELS = 32768


class Arena:
    def __init__(self, t, total):
        self.t = t
        self.total = total
        self.off = 0

    def mark(self):
        return self.off

    def reset(self, m):
        self.off = m

    def take(self, shape):
        n = int(np.prod(shape[1:]))
        o = self.off
        assert o + n <= self.total, f"arena overflow {o}+{n}>{self.total}"
        self.off = o + n
        ap = self.t[0:shape[0], o:o + n]
        if len(shape) == 2:
            return ap
        if len(shape) == 3:
            return ap.rearrange("p (a b) -> p a b", a=shape[1])
        if len(shape) == 4:
            return ap.rearrange("p (a b c) -> p a b c", a=shape[1], b=shape[2])
        raise ValueError(shape)


class MK:

    NATIVE = {"vector": "ev", "scalar": "sc"}
    SERIAL = {"aw0", "aw1", "kv0", "kv1", "vv0", "vv1", "wd2"}

    def __init__(self, nc):
        self.nc = nc
        self.plan = {e: [] for e in ENGS}
        self.cnt = {}
        self.sems = {}
        self.last_selfwait = {}

    def E(self, eng, fn):
        self.plan[eng].append(fn)

    def inc(self, sem, n):
        self.cnt[sem] = self.cnt.get(sem, 0) + n
        return self.cnt[sem]

    def cur(self, sem):
        return self.cnt.get(sem, 0)

    def wait(self, eng, sem, val):
        if val and val > 0:
            if self.NATIVE.get(eng) == sem:
                if val <= self.last_selfwait.get(eng, 0):
                    return
                self.last_selfwait[eng] = val
            self.E(eng, lambda e, s=sem, v=val: e.wait_ge(self.sems[s], v))

    def dma(self, eng, out, in_, sem):
        if sem in self.SERIAL:
            self.wait(eng, sem, self.cur(sem))
        v = self.inc(sem, 16)
        self.E(eng, lambda e, o=out, i=in_, s=sem:
               e.dma_start(out=o, in_=i).then_inc(self.sems[s], 16))
        return v

    def idma(self, out, in_, idx, sem):
        if sem in self.SERIAL:
            self.wait("gpsimd", sem, self.cur(sem))
        v = self.inc(sem, 16)
        self.E("gpsimd", lambda e, o=out, i=in_, x=idx, s=sem:
               e.indirect_dma_start(o, None, i, IndirectOffsetOnAxis(x, 0))
               .then_inc(self.sems[s], 16))
        return v

    def op(self, eng, meth, *args, sem=None, **kw):
        nat = self.NATIVE.get(eng)
        if nat is not None:
            cv = self.cur(nat)
            if cv > self.last_selfwait.get(eng, 0):
                self.wait(eng, nat, cv)
            if sem is None:
                sem = nat
        v = self.inc(sem, 1) if sem else None
        def fn(e, m=meth, a=args, s=sem, k=kw):
            ins = getattr(e, m)(*a, **k)
            if s:
                ins.then_inc(self.sems[s], 1)
        self.E(eng, fn)
        return v

    def mm(self, out, lhsT, rhs, start, stop, sem=None):
        v = self.inc(sem, 1) if sem else None
        def fn(e, o=out, l=lhsT, r=rhs, st=start, sp=stop, s=sem):
            ins = e.matmul(o, l, r, start=st, stop=sp)
            if s:
                ins.then_inc(self.sems[s], 1)
        self.E("tensor", fn)
        return v

    def transpose(self, out, in_, ident, sem):
        v = self.inc(sem, 1)
        self.E("tensor", lambda e, o=out, i=in_, d=ident, s=sem:
               e.transpose(o, i, d).then_inc(self.sems[s], 1))
        return v


def build_program(stages=("ffn1", "attn", "ffn2")):
    nc = bass.Bass("TRN2", target_bir_lowering=False, debug=False)
    dr = {}
    dr["xt"] = nc.dram_tensor("xt", [DC, 128, NT], F32, kind="ExternalInput").ap()
    for nm in ("wq", "wk", "wga", "wp"):
        dr[nm] = nc.dram_tensor(nm, [16, 128, 16, 128], F32, kind="ExternalInput").ap()
    dr["wv"] = nc.dram_tensor("wv", [4, 16, 128, 512], F32, kind="ExternalInput").ap()
    dr["wg"] = nc.dram_tensor("wg", [HC, 128, DC, 128], F32, kind="ExternalInput").ap()
    dr["wu"] = nc.dram_tensor("wu", [HC, 128, DC, 128], F32, kind="ExternalInput").ap()
    dr["wd"] = nc.dram_tensor("wd", [HC, 128, D], F32, kind="ExternalInput").ap()
    dr["mask"] = nc.dram_tensor("mask", [128, 2, 256], F32, kind="ExternalInput").ap()
    dr["kvidx"] = nc.dram_tensor("kvidx", [128, NPOS, 16, 2], I32, kind="ExternalInput").ap()
    dr["ident"] = nc.dram_tensor("ident", [128, 128], F32, kind="ExternalInput").ap()
    dr["out"] = nc.dram_tensor("out", [DC, 128, NT], F32, kind="ExternalOutput").ap()
    dr["cc_in"] = nc.dram_tensor("cc_in", [8 * 128, 4096], BF16, kind="Internal").ap()
    dr["cc_out"] = nc.dram_tensor("cc_out", [CC_ROWS, 4096], BF16, kind="Internal").ap()
    dr["wcg"] = nc.dram_tensor("wcg", [HC, 128, D], BF16, kind="Internal").ap()
    dr["wcu"] = nc.dram_tensor("wcu", [HC, 128, D], BF16, kind="Internal").ap()
    dr["wcd"] = nc.dram_tensor("wcd", [HC, 128, D], BF16, kind="Internal").ap()
    dr["qsc"] = nc.dram_tensor("qsc", [128, DC, NT], BF16, kind="Internal").ap()
    dr["gsc"] = nc.dram_tensor("gsc", [128, DC, NT], BF16, kind="Internal").ap()

    es = ExitStack()
    sb = {}

    def alloc(name, shape, dt):
        sb[name] = es.enter_context(nc.sbuf_tensor(name, shape, dt))
        return sb[name]

    alloc("X", [128, DC, NT], F32)
    alloc("QN", [128, DC, NT], BF16)
    alloc("onesb", [128, 1], BF16)
    alloc("ones1", [1, 128], F32)
    alloc("zbf", [128, 128], BF16)
    alloc("CONSTP", [128, 8], F32)
    alloc("CONSTR", [1, 12], F32)
    alloc("MASK", [128, 2, 256], F32)
    alloc("IDX", [128, NPOS, 16, 2], I32)
    alloc("GS", [128, 2, 512], BF16)
    alloc("IDB", [128, 128], F32)
    alloc("AF", [128, A32_ELS], F32)
    alloc("AB", [128, ABF_ELS], BF16)
    a32 = Arena(sb["AF"], A32_ELS)
    abf = Arena(sb["AB"], ABF_ELS)

    mk = MK(nc)
    op, dma, mm, wait, cur = mk.op, mk.dma, mk.mm, mk.wait, mk.cur
    X, QN = sb["X"], sb["QN"]
    PC = {"C": 0, "mC": 1, "neg1": 2, "half": 3, "cliplo": 4, "cliphi": 5}
    RC = {"invn": 0, "e2m": 1, "e2a": 2, "efm": 3, "efa": 4, "c127": 5, "i127": 6,
          "neg1r": 7}
    cp = lambda k: sb["CONSTP"][:, PC[k]:PC[k] + 1]
    c1 = lambda k: sb["CONSTR"][0:1, RC[k]:RC[k] + 1]

    for j in range(4):
        dma("sync", X[:, 4 * j:4 * j + 4, :], dr["xt"][4 * j:4 * j + 4].transpose([1, 0, 2]), "xl")
    dma("sync", sb["MASK"][:], dr["mask"], "xl")
    x_loaded = dma("sync", sb["IDX"][:], dr["kvidx"], "xl")
    idb_loaded = dma("gpsimd", sb["IDB"][:], dr["ident"], "il")
    op("vector", "memset", sb["onesb"][:], 1.0)
    op("vector", "memset", sb["ones1"][:], 1.0)
    op("vector", "memset", sb["zbf"][:], 0.0)
    for k, vv in (("C", CMAGIC), ("mC", -CMAGIC), ("neg1", -1.0), ("half", 0.5),
                  ("cliplo", -128.0), ("cliphi", 127.0)):
        op("vector", "memset", cp(k), vv)
    for k, vv in (("invn", 1.0 / D), ("e2m", 1.0 + EPS2), ("e2a", EPS1 * EPS2),
                  ("efm", 1.0 + EPSF), ("efa", EPS1 * EPSF),
                  ("c127", 127.0), ("i127", 1.0 / 127.0), ("neg1r", -1.0)):
        op("vector", "memset", c1(k), vv)
    op("vector", "memset", sb["GS"][:], 0.0, sem="ev")

    pb = [es.enter_context(nc.psum_tensor(f"pb{k}", [128, 512], F32)) for k in range(8)]

    def norm_phase(kind, ready, chunk_ready=None):
        m32, mbf = a32.mark(), abf.mark()
        XB = abf.take([128, 2, NT])
        X2B = abf.take([128, 2, NT])
        MACC = abf.take([128, 2, NT])
        ROWS = a32.take([1, 4, NT])
        TQ = a32.take([128, NT])
        SRb = a32.take([128, NT])
        MSRb = a32.take([128, NT])
        IVb = a32.take([128, NT])
        MCOL = a32.take([128, 16])
        M16 = a32.take([16, 128])
        ffn = kind == "ffn"
        if chunk_ready is None:
            for e in ENGS:
                for sm, vl in ready:
                    wait(e, sm, vl)
        else:
            for sm, vl in ready:
                if sm not in ("ev", "sc", "mmS"):
                    wait("vector", sm, vl)
                    wait("scalar", sm, vl)
        stat_mm, tree_ev = {}, {}
        for i in range(DC):
            par = i % 2
            if chunk_ready is not None:
                wait("vector", chunk_ready[i][0], chunk_ready[i][1])
                wait("scalar", chunk_ready[i][0], chunk_ready[i][1])
            if i >= 2:
                wait("vector", "mmS", stat_mm[i - 2])
                wait("scalar", "mmS", stat_mm[i - 2])
            cast_ev = op("vector", "tensor_copy", XB[:, par, :], X[:, i, :])
            sq_sc = op("scalar", "activation", X2B[:, par, :], X[:, i, :], AF.Square)
            wait("tensor", "ev", cast_ev)
            wait("tensor", "sc", sq_sc)
            st, sp = i == 0, i == DC - 1
            mm(pb[0][0:1, :], sb["onesb"][:], XB[:, par, 0:512], st, sp)
            mm(pb[1][0:1, :], sb["onesb"][:], XB[:, par, 512:1024], st, sp)
            mm(pb[2][0:1, :], sb["onesb"][:], X2B[:, par, 0:512], st, sp)
            stat_mm[i] = mm(pb[3][0:1, :], sb["onesb"][:], X2B[:, par, 512:1024], st, sp, sem="mmS")
            if ffn:
                if i == 0:
                    op("vector", "tensor_copy", MACC[:, 0, :], XB[:, par, :])
                    tree_ev[i] = op("vector", "tensor_copy", MACC[:, 1, :], XB[:, par, :])
                else:
                    op("vector", "tensor_tensor", MACC[:, 0, :], MACC[:, 0, :],
                       XB[:, par, :], ALU.max)
                    tree_ev[i] = op("vector", "tensor_tensor", MACC[:, 1, :], MACC[:, 1, :],
                                    XB[:, par, :], ALU.min)
        Rm, Rr, Ra, Rb_ = 0, 1, 2, 3
        r = lambda k: ROWS[0:1, k, :]
        last_stat = stat_mm[DC - 1]
        wait("scalar", "mmS", last_stat)
        op("scalar", "copy", r(Ra)[:, 0:512], pb[0][0:1, :])
        op("scalar", "copy", r(Ra)[:, 512:1024], pb[1][0:1, :])
        op("scalar", "copy", r(Rb_)[:, 0:512], pb[2][0:1, :])
        sc_st = op("scalar", "copy", r(Rb_)[:, 512:1024], pb[3][0:1, :])
        wait("vector", "sc", sc_st)
        op("vector", "tensor_scalar_mul", r(Rm), r(Ra), c1("invn"))
        op("vector", "tensor_scalar_mul", r(Rb_), r(Rb_), c1("invn"))
        op("vector", "tensor_mul", r(Ra), r(Rm), r(Rm))
        op("vector", "tensor_sub", r(Rb_), r(Rb_), r(Ra))
        vv = op("vector", "tensor_scalar", r(Rb_), r(Rb_),
                c1("e2m" if ffn else "efm"), c1("e2a" if ffn else "efa"),
                ALU.mult, ALU.add)
        wait("scalar", "ev", vv)
        sq2 = op("scalar", "activation", r(Ra), r(Rb_), AF.Sqrt)
        wait("vector", "sc", sq2)
        op("vector", "reciprocal", r(Rr), r(Ra))
        if ffn:
            wait("tensor", "il", idb_loaded)
            for w in range(2):
                tcast = op("vector", "tensor_copy", TQ[:], MACC[:, w, :])
                wait("tensor", "ev", tcast)
                for j in range(8):
                    tg = mk.transpose(pb[6 + w][:, 0:128], TQ[:, j * 128:(j + 1) * 128],
                                      sb["IDB"][:], "mmS")
                    wait("vector", "mmS", tg)
                    rd = op("vector", "tensor_reduce", MCOL[:, w * 8 + j:w * 8 + j + 1],
                            pb[6 + w][:, 0:128], mybir.AxisListType.X,
                            ALU.max if w == 0 else ALU.min)
                    wait("tensor", "ev", rd)
            tg = mk.transpose(pb[6][0:16, 0:128], MCOL[:], sb["IDB"][:], "mmS")
            wait("vector", "mmS", tg)
            cx = op("vector", "tensor_copy", M16[:], pb[6][0:16, 0:128])
            wait("sync", "ev", cx)
            dma("sync", r(Ra), M16[0:8, :], "xl")
            rr_ld = dma("sync", r(Rb_), M16[8:16, :], "xl")
            wait("vector", "xl", rr_ld)
            op("vector", "tensor_sub", r(Ra), r(Ra), r(Rm))
            op("vector", "tensor_scalar_mul", r(Rb_), r(Rb_), c1("neg1r"))
            op("vector", "tensor_add", r(Rb_), r(Rb_), r(Rm))
            op("vector", "tensor_max", r(Ra), r(Ra), r(Rb_))
            op("vector", "tensor_mul", r(Ra), r(Ra), r(Rr))
            op("vector", "reciprocal", r(Rb_), r(Ra))
            op("vector", "tensor_scalar_mul", r(Rb_), r(Rb_), c1("c127"))
            op("vector", "tensor_mul", r(Rb_), r(Rb_), r(Rr))
            op("vector", "tensor_mul", r(Rm), r(Rm), r(Rb_))
            op("vector", "tensor_scalar_mul", r(Ra), r(Ra), c1("i127"))
            brows = [(Rb_, SRb), (Rm, MSRb), (Ra, IVb)]
        else:
            op("vector", "tensor_mul", r(Rm), r(Rm), r(Rr))
            brows = [(Rr, SRb), (Rm, MSRb)]
        rows_done = cur("ev")
        wait("tensor", "ev", rows_done)
        for rk, dst in brows:
            mm(pb[6][:], sb["ones1"][:], ROWS[0:1, rk, 0:512], True, True)
            g = mm(pb[7][:], sb["ones1"][:], ROWS[0:1, rk, 512:1024], True, True, sem="mmS")
            wait("vector", "mmS", g)
            op("vector", "tensor_copy", dst[:, 0:512], pb[6][:])
            bev = op("vector", "tensor_copy", dst[:, 512:1024], pb[7][:])
            wait("tensor", "ev", bev)
        last = None
        for i in range(DC):
            if ffn:
                op("vector", "tensor_mul", TQ[:], X[:, i, :], SRb[:])
                tev = op("vector", "tensor_sub", TQ[:], TQ[:], MSRb[:])
                wait("scalar", "ev", tev)
                op("scalar", "activation", TQ[:], TQ[:], AF.Identity, bias=cp("C"))
                tsc = op("scalar", "activation", TQ[:], TQ[:], AF.Identity, bias=cp("mC"))
                wait("vector", "sc", tsc)
                op("vector", "tensor_scalar", TQ[:], TQ[:],
                   cp("cliplo"), cp("cliphi"), ALU.max, ALU.min)
                last = op("vector", "tensor_tensor", QN[:, i, :], TQ[:], IVb[:], ALU.mult)
            else:
                op("vector", "tensor_mul", TQ[:], X[:, i, :], SRb[:])
                last = op("vector", "tensor_sub", QN[:, i, :], TQ[:], MSRb[:])
        a32.reset(m32)
        abf.reset(mbf)
        return [("ev", last)]

    def ffn_main(ready, first_pass, cache_cw=None):
        m32, mbf = a32.mark(), abf.mark()
        if first_pass:
            WGST = a32.take([128, DC, 128])
            WUST = a32.take([128, DC, 128])
            WDST = a32.take([128, 2, D])
        SWG = abf.take([128, 2, DC, 128])
        SWU = abf.take([128, 2, DC, 128])
        SWD = abf.take([128, SBH, D])
        P = abf.take([128, SBH, 2, 512])
        wl_h, sgn_gu, sgn_d, upG, upU, pmul = {}, {}, {}, {}, {}, {}
        wd_h, cwr, cwd = {}, {}, {}
        down_mm, xadd = {}, {}
        for e in ENGS:
            for sm, vl in ready:
                wait(e, sm, vl)
        if not first_pass and cache_cw is not None:
            wait("sync", "cw0", cache_cw[0])
            wait("sync", "cw1", cache_cw[1])
            wait("gpsimd", "cwd", cache_cw[2])
        last_x = None
        for h in range(HC):
            sbi, hj = h // SBH, h % SBH
            par = h % 2
            wlS = "wl0" if par == 0 else "wl1"
            cwS = "cw0" if par == 0 else "cw1"
            if first_pass:
                if h >= 1:
                    wait("sync", "sc", sgn_gu[h - 1])
                dma("sync", WGST[:], dr["wg"][h], wlS)
                wl_h[h] = dma("sync", WUST[:], dr["wu"][h], wlS)
                if h >= 2:
                    wait("gpsimd", "sc", sgn_d[h - 2])
                wd_h[h] = dma("gpsimd", WDST[:, par], dr["wd"][h], "wd2")
                if h >= 1:
                    wait("gpsimd", "sc", sgn_d[h - 1])
                    cwd[h - 1] = dma("gpsimd", dr["wcd"][h - 1], SWD[:, (h - 1) % SBH, :], "cwd")
                wait("scalar", wlS, wl_h[h])
                if h >= 2:
                    wait("scalar", "mmS", upU[h - 2])
                    wait("scalar", cwS, cwr[h - 2])
                op("scalar", "sign", SWG[:, par].rearrange("p a b -> p (a b)"),
                   WGST[:].rearrange("p a b -> p (a b)"))
                sgn_gu[h] = op("scalar", "sign", SWU[:, par].rearrange("p a b -> p (a b)"),
                               WUST[:].rearrange("p a b -> p (a b)"))
                wait("scalar", "wd2", wd_h[h])
                if hj == 0 and sbi >= 1:
                    wait("scalar", "mmS", down_mm[sbi - 1])
                if h >= 1:
                    wait("scalar", "cwd", cwd.get(h - 1, 0))
                sgn_d[h] = op("scalar", "sign", SWD[:, hj, :], WDST[:, par])
                wait("sync", "sc", sgn_gu[h])
                dma("sync", dr["wcg"][h], SWG[:, par].rearrange("p a b -> p (a b)"), cwS)
                cwr[h] = dma("sync", dr["wcu"][h], SWU[:, par].rearrange("p a b -> p (a b)"), cwS)
            else:
                if h >= 2:
                    wait("sync", "mmS", upU[h - 2])
                dma("sync", SWG[:, par].rearrange("p a b -> p (a b)"), dr["wcg"][h], wlS)
                wl_h[h] = dma("sync", SWU[:, par].rearrange("p a b -> p (a b)"), dr["wcu"][h], wlS)
                if hj == 0:
                    if sbi >= 1:
                        wait("gpsimd", "mmS", down_mm[sbi - 1])
                    for j2 in range(SBH):
                        wd_h[sbi] = dma("gpsimd", SWD[:, j2, :], dr["wcd"][sbi * SBH + j2], "wd2")
            if first_pass:
                wait("tensor", "sc", sgn_gu[h])
            else:
                wait("tensor", wlS, wl_h[h])
            for dc in range(DC):
                for th in range(2):
                    if h >= 1 and dc == 0:
                        wait("tensor", "ev", pmul[(h - 1, th)])
                    upG[(h, th)] = mm(pb[0 + th][:], SWG[:, par, dc, :],
                                      QN[:, dc, th * 512:th * 512 + 512],
                                      dc == 0, dc == DC - 1,
                                      sem="mmS" if dc == DC - 1 else None)
            for dc in range(DC):
                for th in range(2):
                    upU[h] = mm(pb[2 + th][:], SWU[:, par, dc, :],
                                QN[:, dc, th * 512:th * 512 + 512],
                                dc == 0, dc == DC - 1,
                                sem="mmS" if dc == DC - 1 else None)
            for th in range(2):
                wait("scalar", "mmS", upG[(h, th)])
                if h >= 1:
                    wait("scalar", "ev", pmul[(h - 1, th)])
                gsil = op("scalar", "activation", sb["GS"][:, th, :], pb[0 + th][:], AF.Silu)
                wait("vector", "sc", gsil)
                wait("vector", "mmS", upU[h])
                if sbi >= 1 and hj == 0:
                    wait("vector", "mmS", down_mm[sbi - 1])
                pmul[(h, th)] = op("vector", "tensor_tensor", P[:, hj, th, :],
                                   sb["GS"][:, th, :], pb[2 + th][:], ALU.mult)
            if hj < SBH - 1:
                continue
            if first_pass:
                wait("tensor", "sc", sgn_d[h])
            else:
                wait("tensor", "wd2", wd_h[sbi])
            prev_grp = None
            for dq in range(4):
                for th in range(2):
                    wait("tensor", "ev", pmul[(sbi * SBH + SBH - 1, th)])
                    if prev_grp is not None:
                        wait("tensor", "ev", prev_grp)
                    elif sbi >= 1:
                        wait("tensor", "ev", xadd[(sbi - 1, 3, 1)])
                    g = None
                    for hj2 in range(SBH):
                        for j in range(4):
                            g = mm(pb[4 + j][:],
                                   SWD[:, hj2, (dq * 4 + j) * 128:(dq * 4 + j) * 128 + 128],
                                   P[:, hj2, th, :], hj2 == 0, hj2 == SBH - 1,
                                   sem="mmS" if (hj2 == SBH - 1 and j == 3) else None)
                    wait("vector", "mmS", g)
                    for j in range(4):
                        xadd[(sbi, dq, th)] = op(
                            "vector", "scalar_tensor_tensor",
                            X[:, dq * 4 + j, th * 512:th * 512 + 512], pb[4 + j][:],
                            cp("half"), X[:, dq * 4 + j, th * 512:th * 512 + 512],
                            ALU.mult, ALU.add)
                    prev_grp = xadd[(sbi, dq, th)]
            down_mm[sbi] = cur("mmS")
            last_x = xadd[(sbi, 3, 1)]
        if first_pass:
            wait("gpsimd", "sc", sgn_d[HC - 1])
            cwd[HC - 1] = dma("gpsimd", dr["wcd"][HC - 1], SWD[:, SBH - 1, :], "cwd")
        a32.reset(m32)
        abf.reset(mbf)
        chunks = [("ev", xadd[(NSB - 1, i // 4, 1)]) for i in range(DC)]
        return [("ev", last_x), ("cw0", cur("cw0")), ("cw1", cur("cw1")),
                ("cwd", cur("cwd"))], chunks

    def attn_phase(ready, chunks_in):
        qa = norm_phase("attn", ready, chunk_ready=chunks_in)

        m32p, mbfp = a32.mark(), abf.mark()
        WPRJ = abf.take([128, 2, 16, 128])
        KVW = abf.take([128, DC, NT])
        WPRJV = abf.take([128, 2, 8, 512])
        KVWf = KVW.rearrange("p a b -> p (a b)")

        for e in ENGS:
            for sm, vl in qa:
                wait(e, sm, vl)
        zev = op("vector", "memset", KVW[:, 0:4, :], 0.0)
        wait("sync", "ev", zev)
        zcw = dma("sync", dr["cc_out"][8 * CC_RANK_ROWS:8 * CC_RANK_ROWS + 128, :], KVW[:, 0:4, :], "cw")

        def wproj(name, dst, func, bank0, pre_mm=None, pre_sc=None, pre_load=None):
            pmm, psc = {}, {-1: 0, -2: 0}
            for oc in range(16):
                par = oc % 2
                if oc >= 2:
                    wait("gpsimd", "mmS", pmm[oc - 2])
                elif pre_load is not None:
                    wait("gpsimd", pre_load[0], pre_load[1])
                aw = dma("gpsimd", WPRJ[:, par], dr[name][oc],
                         "aw0" if par == 0 else "aw1")
                wait("tensor", "aw0" if par == 0 else "aw1", aw)
                if oc == 0 and pre_mm is not None:
                    wait("tensor", pre_mm[0], pre_mm[1])
                if oc >= 2:
                    wait("tensor", "sc", psc[oc - 2])
                g = None
                for th in range(2):
                    bank = bank0 + par * 2 + th
                    for dc in range(DC):
                        g = mm(pb[bank][:], WPRJ[:, par, dc, :],
                               QN[:, dc, th * 512:th * 512 + 512], dc == 0, dc == DC - 1,
                               sem="mmS" if (th == 1 and dc == DC - 1) else None)
                pmm[oc] = g
                wait("scalar", "mmS", g)
                if oc == 0 and pre_sc is not None:
                    wait("scalar", pre_sc[0], pre_sc[1])
                for th in range(2):
                    bank = bank0 + par * 2 + th
                    psc[oc] = op("scalar", "activation", dst[:, oc, th * 512:th * 512 + 512],
                                 pb[bank][:], func)
            return pmm[15], psc[15]

        kmm, ksc = wproj("wk", KVW, AF.Silu, 0, pre_sc=("cw", zcw))
        wait("sync", "sc", ksc)
        kcw = None
        for c in range(4):
            kcw = dma("sync", dr["cc_in"][c * 128:(c + 1) * 128, :],
                      KVW[:, :, c * 256:(c + 1) * 256], "cw")
        wait("gpsimd", "cw", kcw)
        ccv_k = mk.inc("cc", 1)
        mk.E("gpsimd", lambda e: e.collective_compute(
            "AllGather", ALU.bypass, ins=[dr["cc_in"][0:512, :]],
            outs=[dr["cc_out"][0:4 * CC_RANK_ROWS, :]],
            replica_groups=[[0, 1, 2, 3], [4, 5, 6, 7]]).then_inc(mk.sems["cc"], 1))
        vmm, vsc = {}, {}
        for eg in range(4):
            for hh in range(2):
                if (eg, hh) != (0, 0):
                    prev = vmm[(eg, hh - 1)] if hh else vmm[(eg - 1, 1)]
                    wait("gpsimd", "mmS", prev)
                aw = dma("gpsimd", WPRJV[:, hh], dr["wv"][eg, hh * 8:hh * 8 + 8].transpose([1, 0, 2]),
                         "aw0" if hh == 0 else "aw1")
                wait("tensor", "aw0" if hh == 0 else "aw1", aw)
                if hh == 0:
                    wait("tensor", "sc", vsc[eg - 1] if eg else ksc)
                g = None
                for dcs in range(8):
                    dc = hh * 8 + dcs
                    for t in range(8):
                        g = mm(pb[t][:], QN[:, dc, t * 128:(t + 1) * 128], WPRJV[:, hh, dcs, :],
                               dc == 0, dc == 15,
                               sem="mmS" if (dcs == 7 and t == 7) else None)
                vmm[(eg, hh)] = g
            wait("scalar", "mmS", vmm[(eg, 1)])
            if eg == 0:
                wait("scalar", "cw", kcw)
            for t in range(8):
                c, u = t // 2, t % 2
                off = ((c * 2 + u) * 4 + eg) * 512
                vsc[eg] = op("scalar", "activation", KVWf[:, off:off + 512], pb[t][:], AF.Silu)
        wait("sync", "sc", vsc[3])
        vcw = None
        for c in range(4):
            vcw = dma("sync", dr["cc_in"][(4 + c) * 128:(5 + c) * 128, :],
                      KVWf[:, c * 4096:(c + 1) * 4096], "cw")
        wait("gpsimd", "cw", vcw)
        ccv = mk.inc("cc", 1)
        mk.E("gpsimd", lambda e: e.collective_compute(
            "AllGather", ALU.bypass, ins=[dr["cc_in"][512:1024, :]],
            outs=[dr["cc_out"][4 * CC_RANK_ROWS:8 * CC_RANK_ROWS, :]],
            replica_groups=[[0, 1, 2, 3], [4, 5, 6, 7]]).then_inc(mk.sems["cc"], 1))
        qmm, qev = wproj("wq", KVW, AF.Silu, 0, pre_mm=("sc", vsc[3]), pre_load=("mmS", vmm[(3, 1)]))
        wait("sync", "sc", qev)
        qcw = dma("sync", dr["qsc"], KVW[:], "cw")
        gmm, gev = wproj("wga", KVW, AF.Sigmoid, 4, pre_sc=("cw", qcw), pre_load=("mmS", qmm))
        wait("sync", "sc", gev)
        gcw = dma("sync", dr["gsc"], KVW[:], "cw")
        a32.reset(m32p)
        abf.reset(mbfp)

        m32t, mbft = a32.mark(), abf.mark()
        QBs = abf.take([128, 16, 256])
        GBs = abf.take([128, 16, 256])
        KS = abf.take([128, 2, 16, 256])
        VS = abf.take([128, 2, 2, 2048])
        SS = abf.take([128, 16, 2, 256])
        KSf = [KS[:, q2].rearrange("p a b -> p (a b)") for q2 in range(2)]
        VSf = [VS[:, q2].rearrange("p a b -> p (a b)") for q2 in range(2)]
        wait("gpsimd", "cc", ccv_k)
        wait("gpsimd", "cw", gcw)
        wait("sync", "cw", gcw)
        smm, sev, omm = {}, {}, {}
        vload = {}
        oev = {-1: ("sc", gev)}
        first_v = True
        for p in range(NPOS):
            L = SL[p]
            wait("sync", oev[p - 1][0], oev[p - 1][1])
            qb_cw = dma("sync", QBs[:], dr["qsc"][:, :, p * 256:(p + 1) * 256], "cw")
            gb_cw = dma("sync", GBs[:], dr["gsc"][:, :, p * 256:(p + 1) * 256], "cw")
            wait("tensor", "cw", gb_cw)
            for s in range(L):
                par = s % 2
                sb_lo = 4 + 2 * par
                if s >= 2:
                    wait("gpsimd", "mmS", smm[(p, s - 2)])
                kv_k = mk.idma(KSf[par], dr["cc_out"], sb["IDX"][:, p, s, 0:1],
                               "kv0" if par == 0 else "kv1")
                if s < 2:
                    if first_v:
                        wait("gpsimd", "cc", ccv)
                        first_v = False
                    if p >= 1:
                        wait("gpsimd", "mmS", omm[(p - 1, SL[p - 1] - 2 + s)])
                    vload[(p, s)] = mk.idma(VSf[par], dr["cc_out"], sb["IDX"][:, p, s, 1:2],
                                            "vv0" if par == 0 else "vv1")
                wait("tensor", "kv0" if par == 0 else "kv1", kv_k)
                if s < 2:
                    wait("tensor", oev[p - 1][0], oev[p - 1][1])
                else:
                    wait("tensor", sev[(p, s - 2)][0], sev[(p, s - 2)][1])
                g = None
                for ju in range(2):
                    for dc in range(DC):
                        g = mm(pb[sb_lo + ju][:, 0:256], KS[:, par, dc, ju * 128:(ju + 1) * 128],
                               QBs[:, dc, :], dc == 0, dc == DC - 1,
                               sem="mmS" if (ju == 1 and dc == DC - 1) else None)
                smm[(p, s)] = g
                if s == 0:
                    wait("vector", "mmS", g)
                    op("vector", "tensor_tensor", SS[:, s, 0, :], pb[sb_lo][:, 0:256],
                       sb["MASK"][:, 0, :], ALU.mult)
                    e2 = op("vector", "tensor_tensor", SS[:, s, 1, :], pb[sb_lo + 1][:, 0:256],
                            sb["MASK"][:, 1, :], ALU.mult)
                    sev[(p, s)] = ("ev", e2)
                else:
                    wait("scalar", "mmS", g)
                    op("scalar", "copy", SS[:, s, 0, :], pb[sb_lo][:, 0:256])
                    e2 = op("scalar", "copy", SS[:, s, 1, :], pb[sb_lo + 1][:, 0:256])
                    sev[(p, s)] = ("sc", e2)
            wait("tensor", sev[(p, L - 1)][0], sev[(p, L - 1)][1])
            if L >= 2:
                wait("tensor", sev[(p, L - 2)][0], sev[(p, L - 2)][1])
            if p >= 1:
                wait("tensor", oev[p - 1][0], oev[p - 1][1])
            for j in range(8):
                mm(pb[j][:], sb["zbf"][:], sb["GS"][:, 0, :], True, False)
            for s in range(L):
                par = s % 2
                if (p, s) not in vload:
                    wait("gpsimd", "mmS", omm[(p, s - 2)])
                    vload[(p, s)] = mk.idma(VSf[par], dr["cc_out"], sb["IDX"][:, p, s, 1:2],
                                            "vv0" if par == 0 else "vv1")
                wait("tensor", "vv0" if par == 0 else "vv1", vload[(p, s)])
                g = None
                for e in range(16):
                    for u in range(2):
                        g = mm(pb[e // 2][:, (e % 2) * 256:(e % 2) * 256 + 256],
                               VS[:, par, u, e * 128:(e + 1) * 128], SS[:, s, u, :],
                               False, s == L - 1 and u == 1 and e % 2 == 1,
                               sem="mmS" if (e == 15 and u == 1) else None)
                omm[(p, s)] = g
            wait("vector", "mmS", omm[(p, L - 1)])
            wait("vector", "cw", gb_cw)
            e2 = None
            for e in range(16):
                e2 = op("vector", "tensor_tensor", QN[:, e, p * 256:(p + 1) * 256],
                        pb[e // 2][:, (e % 2) * 256:(e % 2) * 256 + 256],
                        GBs[:, e, :], ALU.mult)
            oev[p] = ("ev", e2)
        a32.reset(m32t)
        abf.reset(mbft)

        m32w, mbfw = a32.mark(), abf.mark()
        abf.off = 16384
        WPRJ2 = abf.take([128, 2, 16, 128])
        wpmm, wpev = {}, {}
        wait("gpsimd", oev[3][0], oev[3][1])
        for oc in range(16):
            par = oc % 2
            if oc >= 2:
                wait("gpsimd", "mmS", wpmm[oc - 2])
            aw = dma("gpsimd", WPRJ2[:, par], dr["wp"][oc],
                     "aw0" if par == 0 else "aw1")
            wait("tensor", "aw0" if par == 0 else "aw1", aw)
            if oc == 0:
                wait("tensor", oev[3][0], oev[3][1])
            if oc >= 2:
                wait("tensor", "ev", wpev[oc - 2])
            g = None
            for th in range(2):
                bank = 4 + par * 2 + th
                for ec in range(DC):
                    g = mm(pb[bank][:], WPRJ2[:, par, ec, :], QN[:, ec, th * 512:th * 512 + 512],
                           ec == 0, ec == DC - 1,
                           sem="mmS" if (th == 1 and ec == DC - 1) else None)
            wpmm[oc] = g
            wait("vector", "mmS", g)
            for th in range(2):
                bank = 4 + par * 2 + th
                wpev[oc] = op("vector", "tensor_add", X[:, oc, th * 512:th * 512 + 512],
                              X[:, oc, th * 512:th * 512 + 512], pb[bank][:])
        a32.reset(m32w)
        abf.reset(mbfw)
        return [("ev", wpev[15])], [("ev", wpev[i]) for i in range(DC)]

    barrier = [("xl", x_loaded)]
    xload_chunks = [("xl", x_loaded) for i in range(DC)]
    chunks = None
    ffn1_cw = None
    if "ffn1" in stages:
        q1 = norm_phase("ffn", barrier, chunk_ready=xload_chunks)
        barrier, chunks = ffn_main(q1, True)
        ffn1_cw = (cur("cw0"), cur("cw1"), cur("cw"))
    if "attn" in stages:
        barrier, chunks = attn_phase(barrier, chunks)
    if "ffn2" in stages:
        q2 = norm_phase("ffn", barrier, chunk_ready=chunks)
        barrier, chunks = ffn_main(q2, "ffn1" not in stages, cache_cw=ffn1_cw)

    for sm, vl in barrier:
        wait("sync", sm, vl)
    for j in range(4):
        dma("sync", dr["out"][4 * j:4 * j + 4].transpose([1, 0, 2]), X[:, 4 * j:4 * j + 4, :], "st")
    wait("sync", "st", cur("st"))

    with ExitStack() as ses:
        for s in SEMS:
            mk.sems[s] = ses.enter_context(nc.semaphore(s))
        with nc.Block() as block:
            for eng in ENGS:
                def mkrun(eng):
                    def runner(e):
                        for fn in mk.plan[eng]:
                            fn(e)
                    return runner
                getattr(block, eng)(mkrun(eng))
    es.close()
    return nc


def _chunk_owner():
    own = {}
    for r in range(4):
        for ci, t in enumerate(CSETS[r]):
            own[t] = (r, ci)
    return own


def _host_inputs(L0_attn_x, Wq, Wk, Wv, Wg, Wp, W_g, W_u, W_d):
    prep = {}
    tp = lambda w: np.ascontiguousarray(
        np.asarray(w).reshape(16, 128, 16, 128).transpose(0, 3, 2, 1))
    prep["wq"] = tp(Wq)
    prep["wk"] = tp(Wk)
    prep["wga"] = tp(Wg)
    prep["wp"] = tp(Wp)
    prep["wv"] = np.ascontiguousarray(
        np.asarray(Wv).reshape(4, 512, 16, 128).transpose(0, 2, 3, 1))
    prep["wg"] = np.ascontiguousarray(
        np.asarray(W_g).reshape(HC, 128, DC, 128).transpose(0, 3, 2, 1))
    prep["wu"] = np.ascontiguousarray(
        np.asarray(W_u).reshape(HC, 128, DC, 128).transpose(0, 3, 2, 1))
    prep["wd"] = np.ascontiguousarray(np.asarray(W_d).T.reshape(HC, 128, D))
    prep["ident"] = np.eye(128, dtype=np.float32)
    jj = np.arange(256)[:, None]
    ii = np.arange(256)[None, :]
    prep["mask"] = np.ascontiguousarray(
        (jj <= ii).astype(np.float32).reshape(2, 128, 256).transpose(1, 0, 2))
    own = _chunk_owner()
    x = np.asarray(L0_attn_x)
    in_maps = []
    for c in range(8):
        g, rr = c // 4, c % 4
        rows = np.concatenate([np.arange(t * CH, (t + 1) * CH) for t in CSETS[rr]])
        xs = x[g][rows]
        xt = np.ascontiguousarray(xs.T.reshape(DC, 128, NT))
        kvidx = np.zeros((128, NPOS, 16, 2), np.int32)
        pvec = np.arange(128, dtype=np.int32)
        for p in range(NPOS):
            t = CSETS[rr][p]
            for s in range(16):
                if s == 0:
                    tp_ = t
                elif s - 1 < t and s < SL[p]:
                    tp_ = s - 1
                else:
                    tp_ = -1
                if tp_ < 0:
                    kb = vb = 8 * CC_RANK_ROWS
                else:
                    ro, ci = own[tp_]
                    kb = ro * CC_RANK_ROWS + ci * 128
                    vb = 4 * CC_RANK_ROWS + ro * CC_RANK_ROWS + ci * 128
                kvidx[:, p, s, 0] = kb + pvec
                kvidx[:, p, s, 1] = vb + pvec
        m = dict(prep)
        m["xt"] = xt
        m["kvidx"] = kvidx
        in_maps.append(m)
    return in_maps


_NC_CACHE = {}


def kernel(L0_attn_x, L0_attn_Wq, L0_attn_Wk, L0_attn_Wv, L0_attn_Wg, L0_attn_Wp,
           L0_ffn_W_g, L0_ffn_W_u, L0_ffn_W_d, stages=("ffn1", "attn", "ffn2")):
    in_maps = _host_inputs(L0_attn_x, L0_attn_Wq, L0_attn_Wk, L0_attn_Wv,
                           L0_attn_Wg, L0_attn_Wp, L0_ffn_W_g, L0_ffn_W_u, L0_ffn_W_d)
    key = tuple(stages)
    if key not in _NC_CACHE:
        _NC_CACHE[key] = build_program(stages)
    nc = _NC_CACHE[key]
    res = run_bass_kernel_spmd(nc, in_maps, core_ids=list(range(8)))
    out = np.zeros((B, N, D), np.float32)
    for c in range(8):
        g, rr = c // 4, c % 4
        rows = np.concatenate([np.arange(t * CH, (t + 1) * CH) for t in CSETS[rr]])
        oc = np.asarray(res.results[c]["out"]).reshape(D, NT).T
        out[g][rows] = oc
    return out
